# version 02
# baseline (speedup 1.0000x reference)
import math
import numpy as np
from contextlib import ExitStack

import concourse.bass as bass
import concourse.mybir as mybir
from concourse.bass_utils import run_bass_kernel_spmd

F32 = mybir.dt.float32
F32R = mybir.dt.float32r
BF16 = mybir.dt.bfloat16
AF = mybir.ActivationFunctionType
ALU = mybir.AluOpType
AX = mybir.AxisListType

T = 2048
D = 1024
H = 16
DH = 64
NCORE = 8
WIN = 256
DI = 2730
DI2 = 5460
NLAYER = 6
NEG = -30000.0

NTT = T // 128
NTC = T // 512

_CFG = dict(n_inner=NLAYER, do_outer=True, skip_attn=False, skip_ffn=False, taps=0, ydump=False, nodiag=False)

PV_OUT_BQKV = 0
PV_OUT_BPROJ = 24
PV_LAYER0 = 32
PV_L_STRIDE = 72
PV_CONST = PV_LAYER0 + NLAYER * PV_L_STRIDE
PV_C24 = PV_CONST
PV_C15 = PV_CONST + 1
PV_CM05 = PV_CONST + 2
PV_C8 = PV_CONST + 3
PV_C0125 = PV_CONST + 4
PV_P05 = PV_CONST + 5
PV_CM1 = PV_CONST + 6
PV_COLS = PV_CONST + 7


def _pv(li):
    return PV_LAYER0 + li * PV_L_STRIDE



def _host_tables():
    t = {}
    t["id128"] = np.eye(128, dtype=np.float32)
    perm = np.zeros((128, 128), dtype=np.float32)
    for p_ in range(128):
        perm[p_ ^ 32, p_] = 1.0
    t["perm64"] = perm
    e2 = np.zeros((128, 2), dtype=np.float32)
    e2[:64, 0] = 1.0
    e2[64:, 1] = 1.0
    t["e2"] = e2
    t["ones_col"] = np.ones((128, 1), dtype=np.float32)
    t["ones128"] = np.ones((128, 128), dtype=np.float32)
    t["ones_row"] = np.ones((1, 128), dtype=np.float32)
    t["zero_row"] = np.zeros((1, 512), dtype=np.float32)
    t["zeros128"] = np.zeros((128, 16), dtype=np.float32)
    t["zerosb"] = np.zeros((128, 256), dtype=np.float32)
    t["e2t"] = np.ascontiguousarray(e2.T)
    t["ones_rowb"] = t["ones_row"]
    t["ones_colb"] = t["ones_col"]

    f64 = np.float64
    pos = np.arange(T, dtype=f64)
    dims = np.arange(128) % 64
    ang = pos[:, None] / (10000.0 ** (np.arange(0, DH, 2, dtype=f64) / DH))
    rope = np.stack([np.cos(ang), np.sin(ang)], axis=-1).reshape(T, DH)
    c_out = rope.T[dims, :]
    sign = np.where(np.arange(128) % 2 == 0, -1.0, 1.0)[:, None]
    d_out = sign * (1.0 - rope.T[dims, :])
    t["otab_c"] = c_out.astype(np.float32)
    t["otab_d"] = d_out.astype(np.float32)
    om = np.zeros((4, 128, 512), dtype=np.float32)
    for m in range(4):
        r = np.arange(128)[:, None]
        j = np.arange(512)[None, :]
        om[m] = np.where(j <= m * 128 + r, 0.0, NEG)
    t["omask"] = om.reshape(4 * 128, 512)
    half = np.arange(0, DH, 2, dtype=f64)
    inv_freq = 1.0 / 10000.0 ** (half / DH)
    fr = pos[:, None] * inv_freq[None, :]
    freqs = np.concatenate([fr, fr], axis=-1)
    base_scale = (half + 0.4 * DH) / (1.4 * DH)
    power = (pos - T // 2) / (WIN // 2)
    scale = base_scale[None, :] ** power[:, None]
    scale = np.concatenate([scale, scale], axis=-1)
    cos_t, sin_t = np.cos(freqs), np.sin(freqs)
    sgn = np.where((np.arange(128) % 64) < 32, -1.0, 1.0)[:, None]
    t["itab_cq"] = (cos_t * scale).T[dims, :].astype(np.float32)
    t["itab_sq"] = ((sin_t * scale).T[dims, :] * sgn).astype(np.float32)
    t["itab_ck"] = (cos_t / scale).T[dims, :].astype(np.float32)
    t["itab_sk"] = ((sin_t / scale).T[dims, :] * sgn).astype(np.float32)
    im = np.zeros((3, 128, 384), dtype=np.float32)
    for kind in range(3):
        r = np.arange(128)[:, None]
        jj = np.arange(384)[None, :]
        valid = (jj >= r) & (jj <= r + 256)
        if kind == 0:
            valid &= jj >= 256
        elif kind == 1:
            valid &= jj >= 128
        im[kind] = np.where(valid, 0.0, NEG)
    t["imask"] = im.reshape(3 * 128, 384)
    return t


def _feat_cols(vec, pad_to=None):
    v = np.asarray(vec, dtype=np.float32).ravel()
    n = v.shape[0]
    if pad_to is not None and n < pad_to:
        v = np.concatenate([v, np.zeros(pad_to - n, dtype=np.float32)])
        n = pad_to
    assert n % 128 == 0
    return np.ascontiguousarray(v.reshape(n // 128, 128).T)


OQKV_PERM = np.concatenate([np.arange(h * 192 + part * 64, h * 192 + part * 64 + 64)
                            for part in range(3) for h in range(H)])


def _build_pvec(ins):
    cols = np.zeros((128, PV_COLS), dtype=np.float32)
    bq = np.asarray(ins["L0_attn_bqkv"], dtype=np.float32)[OQKV_PERM]
    cols[:, PV_OUT_BQKV:PV_OUT_BQKV + 24] = _feat_cols(bq)
    cols[:, PV_OUT_BPROJ:PV_OUT_BPROJ + 8] = _feat_cols(ins["L0_attn_bproj"])
    sw = np.arange(128) ^ 32
    for li in range(NLAYER):
        o = _pv(li)
        cols[:, o:o + 8] = _feat_cols(ins["L0_ffn_ln_g"][li])
        cols[:, o + 8:o + 16] = _feat_cols(ins["L0_ffn_ln_b"][li])
        b1 = np.asarray(ins["L0_ffn_b1"][li], dtype=np.float32)
        cols[:, o + 16:o + 38] = _feat_cols(b1[:DI], pad_to=22 * 128)
        cols[:, o + 38:o + 60] = _feat_cols(b1[DI:], pad_to=22 * 128)
        cols[:, o + 60:o + 68] = _feat_cols(ins["L0_ffn_b2"][li])
        qs = np.tile(np.asarray(ins["L0_ffn_q_scale"][li], dtype=np.float32), 2)
        ks = np.tile(np.asarray(ins["L0_ffn_k_scale"][li], dtype=np.float32), 2)
        cols[:, o + 68] = qs
        cols[:, o + 69] = ks
        cols[:, o + 70] = qs[sw]
        cols[:, o + 71] = ks[sw]
    cols[:, PV_C24] = 1e-24
    cols[:, PV_C15] = 1e-5
    cols[:, PV_CM05] = -0.5
    cols[:, PV_C8] = 8.0
    cols[:, PV_C0125] = 0.125
    cols[:, PV_P05] = 0.5
    cols[:, PV_CM1] = -1.0
    return cols



class Sem:
    def __init__(self, handle):
        self.h = handle
        self.n = 0

    def inc(self, inst, k=1):
        inst.then_inc(self.h, k)
        self.n += k
        return self.n


class Sched:

    def __init__(self):
        self.prev = {}
        self.cur = {}

    def new_pass(self):
        self.prev = self.cur
        self.cur = {}

    def mark(self, key, val):
        self.cur.setdefault(key, []).append(val)

    def at(self, key, idx, default=0):
        lst = self.prev.get(key)
        if lst is None or idx >= len(lst) or idx < 0:
            return default
        return lst[idx]


class P:
    pass


def ceil_div(a, b):
    return (a + b - 1) // b



LAST_P = [None]


def build_nc():
    sched = Sched()
    sched.new_pass()
    _build_once(sched)
    sched.new_pass()
    nc = _build_once(sched)
    return nc





def _build_once(sched):
    p = P()
    p.S = sched
    nc = bass.Bass("TRN2", target_bir_lowering=False, detect_race_conditions=False)
    p.nc = nc
    n_inner = _CFG["n_inner"]

    dr = {}
    dr["x"] = nc.dram_tensor("x", [T, D], F32, kind="ExternalInput").ap()
    dr["oWqkvT"] = nc.dram_tensor("oWqkvT", [D, 3 * D], F32R, kind="ExternalInput").ap()
    dr["brow_v"] = nc.dram_tensor("brow_v", [1, D], BF16, kind="ExternalInput").ap()
    dr["oWprojT"] = nc.dram_tensor("oWprojT", [D, D], F32R, kind="ExternalInput").ap()
    dr["iWqkv"] = nc.dram_tensor("iWqkv", [NLAYER, D, 3 * D], F32R, kind="ExternalInput").ap()
    dr["iWo"] = nc.dram_tensor("iWo", [NLAYER, D, D], F32, kind="ExternalInput").ap()
    dr["iW1"] = nc.dram_tensor("iW1", [NLAYER, D, DI2], F32, kind="ExternalInput").ap()
    dr["iW2"] = nc.dram_tensor("iW2", [NLAYER, DI, D], F32, kind="ExternalInput").ap()
    dr["pvec"] = nc.dram_tensor("pvec", [128, PV_COLS], F32, kind="ExternalInput").ap()
    const_dts = {"id128b": BF16, "id128r": F32R, "perm64": F32R,
                 "e2": BF16, "e2t": BF16, "ones_col": F32R, "ones_row": F32R,
                 "ones_rowb": BF16, "ones_colb": BF16, "zerosb": BF16,
                 "zero_row": BF16, "omask": BF16, "imask": BF16,
                 "otab_c": BF16, "otab_d": BF16, "itab_cq": BF16,
                 "itab_sq": BF16, "itab_ck": BF16, "itab_sk": BF16}
    for nm, shape in [("id128b", [128, 128]), ("id128f", [128, 128]),
                      ("id128r", [128, 128]), ("perm64", [128, 128]),
                      ("e2", [128, 2]), ("e2t", [2, 128]), ("ones_col", [128, 1]),
                      ("ones_row", [1, 128]), ("ones_rowb", [1, 128]),
                      ("ones_colb", [128, 1]), ("zerosb", [128, 256]),
                      ("zero_row", [1, 512]), ("zeros128", [128, 16]),
                      ("otab_c", [128, T]), ("otab_d", [128, T]),
                      ("omask", [4 * 128, 512]), ("imask", [3 * 128, 384]),
                      ("itab_cq", [128, T]), ("itab_sq", [128, T]),
                      ("itab_ck", [128, T]), ("itab_sk", [128, T])]:
        dr[nm] = nc.dram_tensor(nm, shape, const_dts.get(nm, F32),
                                kind="ExternalInput").ap()
    dr["out"] = nc.dram_tensor("out", [T, D], F32, kind="ExternalOutput").ap()
    dr["x1_scr"] = nc.dram_tensor("x1_scr", [8, 128, T], F32R, kind="Internal").ap()
    for ti in range(_CFG["taps"]):
        dr[f"tap{ti}"] = nc.dram_tensor(f"tap{ti}", [8, 128, T], F32,
                                        kind="ExternalOutput").ap()
    p.tapn = [0]

    def tap():
        if p.tapn[0] >= _CFG["taps"]:
            return
        ti = p.tapn[0]
        p.tapn[0] += 1
        zm = p.zmark
        st = p.sem["st"]

        @p.blk.sync
        def _(s):
            s.wait_ge(p.sem["vv"].h, zm)
            for kc in range(8):
                st.inc(s.dma_start(out=dr[f"tap{ti}"][kc],
                                   in_=z_ap(p, kc, 0, T).bitcast(F32)), 16)
            p.ms(f"tap{ti}", st.n)
        @p.blk.vector
        def _(v):
            v.wait_ge(st.h, p.pv(f"tap{ti}"))
    p.tap = tap
    p.dr = dr

    ctx = ExitStack()
    p.ctx = ctx

    def sb(name, shape, dtype):
        return nc.alloc_sbuf_tensor(name, shape, dtype)

    ZCH = 2052
    p.ZCH = ZCH
    p.z = sb("z", [128, 8 * ZCH], F32R)
    p.tab = sb("tabs", [128, 4 * T], BF16)
    p.idb = sb("idb", [128, 128], BF16)
    p.idf = sb("idf", [128, 128], F32)
    p.idr = sb("idr", [128, 128], F32R)
    p.perm = sb("permc", [128, 128], F32R)
    p.e2 = sb("e2c", [128, 2], BF16)
    p.e2t = sb("e2tc", [2, 128], BF16)
    p.ones_col = sb("ones_colc", [128, 1], F32R)
    p.ones_row = sb("ones_rowc", [1, 128], F32R)
    p.ones_rowb = sb("ones_rowbc", [1, 128], BF16)
    p.ones_colb = sb("ones_colbc", [128, 1], BF16)
    p.zerosb = sb("zerosbc", [128, 256], BF16)
    p.zero_row = sb("zero_rowc", [1, 512], BF16)
    p.brow_v = sb("brow_vc", [1, D], BF16)
    p.zeros128 = sb("zeros128c", [128, 16], F32)
    p.omask = sb("omaskc", [128, 4 * 512], BF16)
    p.imask = sb("imaskc", [128, 3 * 384], BF16)
    p.pvec = sb("pvecc", [128, PV_COLS], F32)
    p.stat = sb("stat", [128, 1024], F32)
    p.ddump = sb("ddump_sb", [128, 64], F32)
    p.statb = sb("statb", [128, 512], BF16)

    PH_BYTES = (nc.SBUF_PARTITION_SIZE_BYTES - 512
                - (nc.bump_sbuf(0)[0]))
    p.arena_base = nc.bump_sbuf(PH_BYTES)[0]
    p._arena_n = [0]

    p.debug_names = {}

    def arena(name, shape, dtype, off):
        p._arena_n[0] += 1
        t = nc.alloc_sbuf_tensor_at(f"{name}_{p._arena_n[0]}", shape, dtype,
                                    offset=p.arena_base + off)
        p.debug_names[name] = t.name
        return t
    p.arena = arena

    p.B = [nc.alloc_psum_tensor(f"bank{i}", [128, 512], F32) for i in range(6)]
    p.PTB = [nc.alloc_psum_tensor(f"ptbank{i}", [128, 512], F32) for i in range(2)]

    p.sem = {}
    for nm in ["wh", "wg", "mm", "vv", "ss", "gg", "st"]:
        p.sem[nm] = Sem(ctx.enter_context(nc.semaphore(nm)))

    p.blk = ctx.enter_context(nc.Block())

    def ms(key, val):
        sched.mark(key, val)

    def pv(key, idx=None):
        lst = sched.cur.get(key)
        if not lst:
            return 0
        if idx is None:
            return lst[-1]
        return lst[idx] if 0 <= idx < len(lst) else 0
    p.ms, p.pv = ms, pv
    p.tab_last_read = 0

    emit_load_consts(p)
    emit_load_x(p)
    p.zmark = p.sem["vv"].n
    p.tap()
    if _CFG["do_outer"]:
        emit_outer(p)
        p.tap()
    for li in range(n_inner):
        if not _CFG["skip_attn"]:
            emit_inner_attn(p, li)
            p.tap()
        if not _CFG["skip_ffn"]:
            emit_inner_ffn(p, li)
            p.tap()
    emit_final(p)

    ctx.close()
    LAST_P[0] = p
    return nc


def z_ap(p, c, t0, n, shift=0):
    base = c * p.ZCH + 1 + t0 - shift
    return p.z[:, base:base + n]



def emit_load_consts(p):
    blk, dr = p.blk, p.dr
    wh = p.sem["wh"]

    @blk.sync
    def _(s):
        for dst, src in [(p.idb, dr["id128b"]), (p.idf, dr["id128f"]),
                         (p.idr, dr["id128r"]), (p.perm, dr["perm64"]),
                         (p.e2, dr["e2"]), (p.e2t, dr["e2t"]),
                         (p.ones_col, dr["ones_col"]), (p.ones_row, dr["ones_row"]),
                         (p.ones_rowb, dr["ones_rowb"]), (p.ones_colb, dr["ones_colb"]),
                         (p.zerosb, dr["zerosb"]),
                         (p.zero_row, dr["zero_row"]), (p.zeros128, dr["zeros128"]),
                         (p.brow_v, dr["brow_v"]), (p.pvec, dr["pvec"])]:
            wh.inc(s.dma_start(out=dst[:], in_=src[:]), 16)
        for m in range(4):
            wh.inc(s.dma_start(out=p.omask[:, m * 512:(m + 1) * 512],
                               in_=dr["omask"][m * 128:(m + 1) * 128, :]), 16)
        for m in range(3):
            wh.inc(s.dma_start(out=p.imask[:, m * 384:(m + 1) * 384],
                               in_=dr["imask"][m * 128:(m + 1) * 128, :]), 16)
        wh.inc(s.dma_start(out=p.tab[:, 0:T], in_=dr["otab_c"][:]), 16)
        wh.inc(s.dma_start(out=p.tab[:, T:2 * T], in_=dr["otab_d"][:]), 16)
        p.S.mark("consts_done", wh.n)


def emit_load_x(p):
    blk, dr, S = p.blk, p.dr, p.S
    wh, mm, vv = p.sem["wh"], p.sem["mm"], p.sem["vv"]
    stage = p.arena("x_stage", [128, 2 * D], F32, 0)

    @blk.sync
    def _(s):
        for i in range(NTT):
            if i >= 2:
                s.wait_ge(vv.h, S.at("xld_free", i - 2))
            wh.inc(s.dma_start(out=stage[:, (i % 2) * D:(i % 2) * D + D],
                               in_=dr["x"][i * 128:(i + 1) * 128, :]), 16)
            S.mark("xld", wh.n)

    @blk.tensor
    def _(t):
        for i in range(NTT):
            t.wait_ge(wh.h, S.at("xld", i))
            for c in range(8):
                k = i * 8 + c
                b = p.B[k % 4]
                if k >= 4:
                    t.wait_ge(vv.h, S.at("xtr_evac", k - 4))
                mm.inc(t.transpose(b[:, 0:128].bitcast(F32),
                                   stage[:, (i % 2) * D + c * 128:(i % 2) * D + (c + 1) * 128],
                                   p.idf[:]))
                S.mark("xtr", mm.n)

    @blk.vector
    def _(v):
        for i in range(NTT):
            for c in range(8):
                k = i * 8 + c
                v.wait_ge(mm.h, S.at("xtr", k))
                b = p.B[k % 4]
                vv.inc(v.tensor_copy(z_ap(p, c, i * 128, 128), b[:, 0:128].bitcast(F32)))
                S.mark("xtr_evac", vv.n)
            S.mark("xld_free", vv.n)
        for c in range(8):
            vv.inc(v.tensor_copy(p.z[:, c * p.ZCH:c * p.ZCH + 1], p.zeros128[:, 0:1]))
        S.mark("z_ready", vv.n)



def emit_final(p):
    blk, dr, S = p.blk, p.dr, p.S
    wh, mm, vv, st = p.sem["wh"], p.sem["mm"], p.sem["vv"], p.sem["st"]
    stage = p.arena("out_stage", [128, 2 * D], F32, 80 * 1024)

    if _CFG["do_outer"]:
        x1r = p.arena("fx1", [128, 16384], F32R, 0)
        zzf = p.arena("fzz", [128, 512], F32R, 65536)
        fscr = p.arena("ffs", [128, 512], F32, 67584)
        nmrow = p.arena("fnm", [1, 2048], BF16, 69632)
        rsrow = p.arena("frs", [1, 2048], BF16, 73728)
        zrd = p.zmark

        @blk.sync
        def _(s):
            s.wait_ge(vv.h, zrd)
            for kc in range(8):
                wh.inc(s.dma_start(out=x1r[:, kc * 2048:(kc + 1) * 2048],
                                   in_=dr["x1_scr"][kc]), 16)
            p.ms("fx1ld", wh.n)

        emit_ln_rows(p, "fLf", lambda kc, tc: z_ap(p, kc, tc * 512, 512),
                     zzf, p.ones_col[:],
                     lambda tc: nmrow[0:1, tc * 512:(tc + 1) * 512],
                     lambda tc: rsrow[0:1, tc * 512:(tc + 1) * 512],
                     gate_vv=zrd)
        emit_ln_apply(p, "fLf", "final",
                      lambda tc: nmrow[0:1, tc * 512:(tc + 1) * 512],
                      lambda tc: rsrow[0:1, tc * 512:(tc + 1) * 512], fscr,
                      x1r=x1r, gate_wh=p.pv("fx1ld"))

    zfin = vv.n

    @blk.tensor
    def _(t):
        t.wait_ge(vv.h, zfin)
        for i in range(NTT):
            for c in range(8):
                k = i * 8 + c
                b = p.B[k % 4]
                if k >= 4:
                    t.wait_ge(vv.h, S.at("otr_evac", k - 4))
                mm.inc(t.transpose(b[:, 0:128].bitcast(F32),
                                   z_ap(p, c, i * 128, 128).bitcast(F32),
                                   p.idf[:]))
                S.mark("otr", mm.n)

    @blk.vector
    def _(v):
        for i in range(NTT):
            if i >= 2:
                v.wait_ge(st.h, S.at("ost", i - 2))
            for c in range(8):
                k = i * 8 + c
                v.wait_ge(mm.h, S.at("otr", k))
                b = p.B[k % 4]
                vv.inc(v.tensor_copy(stage[:, (i % 2) * D + c * 128:(i % 2) * D + (c + 1) * 128],
                                     b[:, 0:128].bitcast(F32)))
                S.mark("otr_evac", vv.n)
            S.mark("ordy", vv.n)

    @blk.sync
    def _(s):
        for i in range(NTT):
            s.wait_ge(vv.h, S.at("ordy", i))
            st.inc(s.dma_start(out=dr["out"][i * 128:(i + 1) * 128, :],
                               in_=stage[:, (i % 2) * D:(i % 2) * D + D]), 16)
            S.mark("ost", st.n)
        s.wait_ge(st.h, st.n)



A_V = 0
A_Y = 32768
A_WV = 32768
A_KH = 65536
A_QH = A_KH + 4608
A_S1 = A_QH + 4096
A_S2 = A_S1 + 2048
A_QSB = A_S2 + 2048
A_Q2 = A_QSB + 2048
A_RROW = A_Q2 + 1024
A_PTS = A_RROW + 1024
A_PSB = A_PTS + 2048
A_WQK = A_PSB + 1536
A_WO = A_KH
A_ATT_END = A_WQK + 8192
F_XN = 0
F_PROD = 16384
F_GELU = F_PROD + 45056
F_ZZ = F_GELU + 2048
F_ROWS = F_ZZ + 2048
F_SCR = F_ROWS + 2048
F_W1 = F_SCR + 2048
F_W2 = F_W1 + 8192
F_END = F_W2 + 11264


def wslice(w3, li, col0, ncol, row0=0, nrow=None):
    if nrow is None:
        nrow = w3.shape[1]
    w = w3[li:li + 1, row0:row0 + nrow, :]
    return w.rearrange("l (c p) m -> p (l c) m", p=128)[:, :, col0:col0 + ncol]


def emit_inner_attn(p, li):
    blk, dr, S = p.blk, p.dr, p.S
    wh, wg, mm, vv, ss = (p.sem[k] for k in ["wh", "wg", "mm", "vv", "ss"])
    L = f"L{li}a_"
    ar = p.arena
    o = _pv(li)
    v_buf = ar(L + "v", [128, 16384], BF16, A_V)
    y_buf = ar(L + "y", [128, 16384], BF16, A_Y)
    wv = ar(L + "wv", [128, 4096], F32R, A_WV)
    kh = ar(L + "kh", [128, 2304], BF16, A_KH)
    qh = ar(L + "qh", [128, 2048], BF16, A_QH)
    s1 = ar(L + "s1", [128, 512], F32, A_S1)
    s2 = ar(L + "s2", [128, 512], F32, A_S2)
    qsb = ar(L + "qsb", [128, 512], F32R, A_QSB)
    q2 = ar(L + "q2", [128, 512], BF16, A_Q2)
    rrow = ar(L + "rrow", [2, 512], BF16, A_RROW)
    ptb = ar(L + "ptb", [128, 1024], BF16, A_PTS)
    psb = ar(L + "psb", [128, 768], BF16, A_PSB)
    wqk = ar(L + "wqk", [128, 2048], F32R, A_WQK)
    wo = ar(L + "wo", [128, 2048], BF16, A_WO)
    zrd = p.zmark

    @blk.sync
    def _(s):
        s.wait_ge(vv.h, p.tab_last_read)
        for j, src in enumerate(["itab_cq", "itab_sq", "itab_ck", "itab_sk"]):
            wh.inc(s.dma_start(out=p.tab[:, j * T:(j + 1) * T], in_=dr[src][:]), 16)
        p.ms(L + "tabs", wh.n)
        s.wait_ge(vv.h, zrd)
        for dc in range(2):
            s.wait_ge(mm.h, S.at(L + "wvfree", dc - 1))
            for kc in range(8):
                wh.inc(s.dma_start(
                    out=wv[:, kc * 512:(kc + 1) * 512],
                    in_=dr["iWqkv"][li, kc * 128:(kc + 1) * 128,
                                    2048 + dc * 512:2048 + (dc + 1) * 512]), 16)
            p.ms(L + "wv", wh.n)
        for pr in range(8):
            if pr == 0:
                s.wait_ge(vv.h, zrd)
            s.wait_ge(mm.h, S.at(L + "wqkfree", pr - 1))
            wh.inc(s.dma_start(
                out=wqk[:, 0:1024].rearrange("p (c m) -> p c m", m=128),
                in_=wslice(dr["iWqkv"], li, pr * 128, 128)), 16)
            wh.inc(s.dma_start(
                out=wqk[:, 1024:2048].rearrange("p (c m) -> p c m", m=128),
                in_=wslice(dr["iWqkv"], li, 1024 + pr * 128, 128)), 16)
            p.ms(L + "wqk", wh.n)

    @blk.vector
    def _(v):
        v.wait_ge(wh.h, p.pv(L + "tabs"))
        for j, col in enumerate([o + 68, o + 70, o + 69, o + 71]):
            vv.inc(v.tensor_scalar_mul(p.tab[:, j * T:(j + 1) * T],
                                       p.tab[:, j * T:(j + 1) * T],
                                       p.pvec[:, col:col + 1]))
        p.ms(L + "tabf", vv.n)
        vv.inc(v.tensor_copy(kh[:, 0:256], p.zerosb[:, 0:256]))
        p.ms(L + "khpad", vv.n)

    groups = [(0, 6), (6, 12), (12, 16)]

    @blk.tensor
    def _(t):
        t.wait_ge(vv.h, zrd)
        for dc in range(2):
            t.wait_ge(wh.h, p.pv(L + "wv", dc))
            for gi, (t0, t1) in enumerate(groups):
                idx = dc * 3 + gi
                if idx > 0:
                    t.wait_ge(vv.h, S.at(L + "vev", idx - 1))
                for kc in range(8):
                    for tt in range(t0, t1):
                        inst = t.matmul(p.B[tt - t0][:],
                                        z_ap(p, kc, tt * 128, 128, shift=(kc >= 4)),
                                        wv[:, kc * 512:(kc + 1) * 512],
                                        start=(kc == 0), stop=(kc == 7))
                mm.inc(inst)
                p.ms(L + "vmm", mm.n)
            p.ms(L + "wvfree", mm.n)

    @blk.vector
    def _(v):
        for dc in range(2):
            for gi, (t0, t1) in enumerate(groups):
                idx = dc * 3 + gi
                v.wait_ge(mm.h, p.pv(L + "vmm", idx))
                for tt in range(t0, t1):
                    vv.inc(v.tensor_copy(
                        v_buf[:, tt * 1024 + dc * 512:tt * 1024 + dc * 512 + 512],
                        p.B[tt - t0][:]))
                p.ms(L + "vev", vv.n)
        p.ms(L + "vdone", vv.n)

    for pr in range(8):
        emit_qk_pair(p, li, pr, wqk, qh, kh, s1, s2, qsb, q2, rrow)
        emit_attn_pair(p, li, pr, qh, kh, v_buf, y_buf, psb, ptb)
    p.tab_last_read = vv.n
    p.ms(L + "ydone", vv.n)
    p.ms(L + "khfree", mm.n)

    if _CFG["ydump"] and li == 0:
        ydr = p.nc.dram_tensor("ydump", [128, 16384], BF16, kind="ExternalOutput").ap()
        ddr = p.nc.dram_tensor("ddump", [128, 64], F32, kind="ExternalOutput").ap()
        vdr = p.nc.dram_tensor("vdump", [128, 16384], BF16, kind="ExternalOutput").ap()
        st = p.sem["st"]
        ym = p.pv(L + "ydone")

        @blk.sync
        def _(s):
            s.wait_ge(vv.h, ym)
            st.inc(s.dma_start(out=ydr[:], in_=y_buf[:]), 16)
            st.inc(s.dma_start(out=vdr[:], in_=v_buf[:]), 16)
            st.inc(s.dma_start(out=ddr[:], in_=p.ddump[:]), 16)
            p.ms("ydump", st.n)

        @blk.vector
        def _(v):
            v.wait_ge(st.h, p.pv("ydump"))

    @blk.gpsimd
    def _(g):
        g.wait_ge(mm.h, p.pv(L + "khfree"))
        for dout in range(8):
            g.wait_ge(mm.h, S.at(L + "wodone", dout - 2))
            wg.inc(g.dma_start(
                out=wo[:, (dout % 2) * 1024:(dout % 2) * 1024 + 1024]
                    .rearrange("p (c m) -> p c m", m=128),
                in_=wslice(dr["iWo"], li, dout * 128, 128)), 16)
            p.ms(L + "wo", wg.n)

    @blk.tensor
    def _(t):
        t.wait_ge(vv.h, p.pv(L + "ydone"))
        t.wait_ge(ss.h, ss.n)
        for dout in range(8):
            t.wait_ge(wg.h, p.pv(L + "wo", dout))
            for tc in range(NTC):
                k6 = (dout * 4 + tc) % 6
                if dout * 4 + tc >= 6:
                    t.wait_ge(vv.h, S.at(L + "woev", dout * 4 + tc - 6))
                for kc in range(8):
                    inst = t.matmul(
                        p.B[k6][:],
                        wo[:, (dout % 2) * 1024 + kc * 128:(dout % 2) * 1024 + kc * 128 + 128],
                        y_buf[:, kc * 2048 + tc * 512:kc * 2048 + tc * 512 + 512],
                        start=(kc == 0), stop=(kc == 7))
                mm.inc(inst)
                p.ms(L + "womm", mm.n)
            p.ms(L + "wodone", mm.n)

    @blk.vector
    def _(v):
        for dout in range(8):
            for tc in range(NTC):
                v.wait_ge(mm.h, p.pv(L + "womm", dout * 4 + tc))
                zsl = z_ap(p, dout, tc * 512, 512)
                vv.inc(v.scalar_tensor_tensor(zsl, p.B[(dout * 4 + tc) % 6][:],
                                              1.0, zsl, ALU.mult, ALU.add))
                p.ms(L + "woev", vv.n)
    p.zmark = vv.n


def emit_qk_pair(p, li, pr, wqk, qh, kh, s1, s2, qsb, q2, rrow):
    blk, S = p.blk, p.S
    wh, mm, vv, ss = (p.sem[k] for k in ["wh", "mm", "vv", "ss"])
    L = f"L{li}a_"
    Lp = f"L{li}a_p{pr}_"
    tiles = [(side, tc) for side in range(2) for tc in range(NTC)]

    @blk.tensor
    def _(t):
        t.wait_ge(wh.h, p.pv(L + "wqk", pr))
        if pr == 0:
            t.wait_ge(vv.h, p.pv(L + "tabf"))
            t.wait_ge(vv.h, p.pv(L + "vdone"))
        else:
            t.wait_ge(vv.h, p.pv(f"L{li}a_p{pr-1}_yevall"))
        for i, (side, tc) in enumerate(tiles):
            bank = p.B[i % 2]
            t.wait_ge(vv.h, S.at(Lp + "qsb", i - 2))
            for kc in range(8):
                inst = t.matmul(bank[:],
                                wqk[:, side * 1024 + kc * 128:side * 1024 + kc * 128 + 128],
                                z_ap(p, kc, tc * 512, 512, shift=(kc >= 4)),
                                start=(kc == 0), stop=(kc == 7))
            mm.inc(inst)
            p.ms(Lp + "qkmm", mm.n)
            t.wait_ge(vv.h, S.at(Lp + "q2", i))
            mm.inc(t.matmul(p.B[2][0:2, :], p.e2[:], q2[:], start=True, stop=True))
            p.ms(Lp + "ssq", mm.n)
            t.wait_ge(vv.h, S.at(Lp + "qsb", i))
            mm.inc(t.matmul(p.B[3][:], p.perm[:], qsb[:], start=True, stop=True))
            p.ms(Lp + "perm", mm.n)
            t.wait_ge(ss.h, S.at(Lp + "rrow", i))
            mm.inc(t.matmul(p.B[4][:], p.e2t[:], rrow[:], start=True, stop=True))
            p.ms(Lp + "rbc", mm.n)
        p.ms(L + "wqkfree", mm.n)

    @blk.vector
    def _(v):
        for i, (side, tc) in enumerate(tiles):
            bank = p.B[i % 2]
            v.wait_ge(mm.h, p.pv(Lp + "qkmm", i))
            vv.inc(v.tensor_copy(qsb[:], bank[:]))
            v.wait_ge(vv.h, vv.n)
            p.ms(Lp + "qsb", vv.n)
            vv.inc(v.tensor_mul(q2[:], qsb[:], qsb[:]))
            p.ms(Lp + "q2", vv.n)
            ct0 = 0 if side == 0 else 2 * T
            st0 = T if side == 0 else 3 * T
            vv.inc(v.tensor_mul(s1[:], qsb[:],
                                p.tab[:, ct0 + tc * 512:ct0 + tc * 512 + 512]))
            v.wait_ge(mm.h, p.pv(Lp + "perm", i))
            vv.inc(v.tensor_mul(s2[:], p.B[3][:],
                                p.tab[:, st0 + tc * 512:st0 + tc * 512 + 512]))
            v.wait_ge(vv.h, vv.n)
            vv.inc(v.tensor_add(s1[:], s1[:], s2[:]))
            v.wait_ge(vv.h, vv.n)
            v.wait_ge(mm.h, p.pv(Lp + "rbc", i))
            if side == 1 and tc == 0:
                v.wait_ge(vv.h, p.pv(L + "khpad"))
            dst = qh[:, tc * 512:(tc + 1) * 512] if side == 0 else \
                kh[:, 256 + tc * 512:256 + (tc + 1) * 512]
            vv.inc(v.tensor_mul(dst, s1[:], p.B[4][:]))
            p.ms(Lp + "qkdone", vv.n)
        p.ms(Lp + "qkall", vv.n)

    @blk.scalar
    def _(s):
        for i, (side, tc) in enumerate(tiles):
            s.wait_ge(mm.h, p.pv(Lp + "ssq", i))
            ss.inc(s.activation(p.stat[0:2, 512:1024], p.B[2][0:2, :], AF.Ln,
                                bias=p.pvec[0:2, PV_C24:PV_C24 + 1]))
            s.wait_ge(ss.h, ss.n)
            ss.inc(s.activation(rrow[:], p.stat[0:2, 512:1024], AF.Exp,
                                scale=p.pvec[0:2, PV_CM05:PV_CM05 + 1]))
            p.ms(Lp + "rrow", ss.n)


def emit_attn_pair(p, li, pr, qh, kh, v_buf, y_buf, psb, ptb):
    blk, S = p.blk, p.S
    mm, vv, ss = (p.sem[k] for k in ["mm", "vv", "ss"])
    Lp = f"L{li}a_p{pr}_"

    @blk.tensor
    def _(t):
        t.wait_ge(vv.h, p.pv(Lp + "qkall"))
        for hh in range(2):
            h0 = hh * 64
            g = pr * 2 + hh
            for tc in range(NTC):
                for h2 in range(2):
                    for qi in range(2):
                        qt = tc * 4 + h2 * 2 + qi
                        G = hh * 16 + qt
                        ib = qt % 2
                        t.wait_ge(ss.h, S.at(Lp + "exp", G - 2))
                        kind = 0 if qt == 0 else (1 if qt == 1 else 2)
                        t.matmul(p.B[4 + ib][:, 0:384],
                                 qh[h0:h0 + 64, qt * 128:(qt + 1) * 128],
                                 kh[h0:h0 + 64, qt * 128:qt * 128 + 384],
                                 start=True, stop=False)
                        mm.inc(t.matmul(p.B[4 + ib][:, 0:384], p.idb[:],
                                        p.imask[:, kind * 384:(kind + 1) * 384],
                                        start=False, stop=True))
                        p.ms(Lp + "smm", mm.n)
                        t.wait_ge(vv.h, S.at(Lp + "diag", G))
                        t.wait_ge(vv.h, S.at(Lp + "ptevq", G - 2))
                        dg = p.idb[:] if _CFG["nodiag"] else \
                            p.statb[:, (qt % 2) * 128:(qt % 2) * 128 + 128]
                        for j3 in range(3):
                            inst = t.matmul(
                                p.PTB[ib][:, j3 * 128:j3 * 128 + 128],
                                psb[:, (qt % 2) * 384 + j3 * 128:(qt % 2) * 384 + j3 * 128 + 128],
                                dg, start=True, stop=True)
                        mm.inc(inst)
                        p.ms(Lp + "pt", mm.n)
                    hidx = tc * 2 + h2
                    HG = hh * 8 + hidx
                    t.wait_ge(vv.h, S.at(Lp + "ptev", HG))
                    t.wait_ge(vv.h, S.at(Lp + "yev", HG - 2))
                    yb = p.B[hidx % 2]
                    tp = (0, h0)
                    yap = yb[h0:h0 + 64, 0:256]
                    t.matmul(yap, p.zero_row[0:1, 0:64], p.zero_row[0:1, 256:512],
                             start=True, stop=False, tile_position=tp)
                    for j in range(4):
                        kb = 2 * h2 + j
                        vt = tc * 4 - 2 + kb
                        if vt < 0:
                            continue
                        lo, hi = (0, 128) if j == 0 else ((128, 256) if j == 3 else (0, 256))
                        t.matmul(
                            yb[h0:h0 + 64, lo:hi],
                            v_buf[:, vt * 1024 + g * 64:vt * 1024 + g * 64 + 64],
                            ptb[:, j * 256 + lo:j * 256 + hi],
                            start=False, stop=False, tile_position=tp)
                    inst = t.matmul(yap, p.zero_row[0:1, 0:64], p.zero_row[0:1, 256:512],
                                    start=False, stop=True, tile_position=tp)
                    mm.inc(inst)
                    p.ms(Lp + "pv", mm.n)

    @blk.vector
    def _(v):
        for hh in range(2):
            h0 = hh * 64
            for tc in range(NTC):
                for h2 in range(2):
                    for qi in range(2):
                        qt = tc * 4 + h2 * 2 + qi
                        G = hh * 16 + qt
                        ib = qt % 2
                        v.wait_ge(mm.h, p.pv(Lp + "smm", G))
                        vv.inc(v.reduce_max(p.stat[:, qt % 4:qt % 4 + 1],
                                            p.B[4 + ib][:, 0:384], axis=AX.X,
                                            negate=True))
                        v.wait_ge(vv.h, vv.n)
                        vv.inc(v.tensor_scalar_mul(p.stat[:, 4 + qt % 4:5 + qt % 4],
                                                   p.stat[:, qt % 4:qt % 4 + 1], 8.0))
                        p.ms(Lp + "bias", vv.n)
                        v.wait_ge(ss.h, S.at(Lp + "rd", G))
                        vv.inc(v.tensor_scalar_mul(
                            p.statb[:, (qt % 2) * 128:(qt % 2) * 128 + 128],
                            p.idb[:], p.stat[:, 12 + (qt % 2):13 + (qt % 2)]))
                        p.ms(Lp + "diag", vv.n)
                        v.wait_ge(mm.h, p.pv(Lp + "pt", G))
                        dstp = ptb.rearrange("k (a b) -> k a b", b=256)[:, qi:qi + 3,
                                                                       qi * 128:qi * 128 + 128]
                        vv.inc(v.tensor_copy(dstp, p.PTB[ib][:, 0:384]
                                             .rearrange("k (a b) -> k a b", b=128)))
                        p.ms(Lp + "ptevq", vv.n)
                        if qi == 1:
                            p.ms(Lp + "ptev", vv.n)
                    hidx = tc * 2 + h2
                    HG = hh * 8 + hidx
                    v.wait_ge(mm.h, p.pv(Lp + "pv", HG))
                    yb = p.B[hidx % 2]
                    vv.inc(v.tensor_copy(
                        y_buf[h0:h0 + 64, pr * 2048 + tc * 512 + h2 * 256:
                              pr * 2048 + tc * 512 + h2 * 256 + 256],
                        yb[h0:h0 + 64, 0:256]))
                    p.ms(Lp + "yev", vv.n)
        p.ms(Lp + "yevall", vv.n)

    @blk.scalar
    def _(s):
        for hh in range(2):
            for tc in range(NTC):
                for h2 in range(2):
                    for qi in range(2):
                        qt = tc * 4 + h2 * 2 + qi
                        G = hh * 16 + qt
                        ib = qt % 2
                        s.wait_ge(vv.h, p.pv(Lp + "bias", G))
                        ss.inc(s.activation(psb[:, (qt % 2) * 384:(qt % 2) * 384 + 384],
                                            p.B[4 + ib][:, 0:384],
                                            AF.Exp, scale=p.pvec[:, PV_C8:PV_C8 + 1],
                                            bias=p.stat[:, 4 + qt % 4:5 + qt % 4],
                                            accum_out=p.stat[:, 8 + qt % 4:9 + qt % 4]))
                        p.ms(Lp + "exp", ss.n)
                        s.wait_ge(ss.h, ss.n)
                        ss.inc(s.activation(p.stat[:, 14:15],
                                            p.stat[:, 8 + qt % 4:9 + qt % 4], AF.Ln))
                        s.wait_ge(ss.h, ss.n)
                        ss.inc(s.activation(p.stat[:, 12 + (qt % 2):13 + (qt % 2)],
                                            p.stat[:, 14:15], AF.Exp,
                                            scale=p.pvec[:, PV_CM1:PV_CM1 + 1]))
                        p.ms(Lp + "rd", ss.n)
                        if _CFG["ydump"] and li == 0 and pr == 0 and G < 16:
                            s.copy(p.ddump[:, G:G + 1],
                                   p.stat[:, 8 + qt % 4:9 + qt % 4])
                            s.copy(p.ddump[:, 16 + G:17 + G],
                                   p.stat[:, 4 + qt % 4:5 + qt % 4])
                            s.copy(p.ddump[:, 32 + G:33 + G],
                                   p.stat[:, qt % 4:qt % 4 + 1])
                            s.copy(p.ddump[:, 48 + G:49 + G],
                                   p.B[4 + ib][:, 0:1])


def emit_inner_ffn(p, li):
    blk, dr, S = p.blk, p.dr, p.S
    wh, wg, mm, vv, ss = (p.sem[k] for k in ["wh", "wg", "mm", "vv", "ss"])
    L = f"L{li}f_"
    ar = p.arena
    o = _pv(li)
    xn = ar(L + "xn", [128, 8192], BF16, F_XN)
    prod = ar(L + "prod", [128, 22528], BF16, F_PROD)
    gelu = ar(L + "gelu", [128, 1024], BF16, F_GELU)
    zz = ar(L + "zz", [128, 512], F32R, F_ZZ)
    rowsb = ar(L + "rows", [1, 1024], BF16, F_ROWS)
    fscr = ar(L + "fscr", [128, 512], F32, F_SCR)
    w1sb = ar(L + "w1", [128, 4096], BF16, F_W1)
    w2sb = ar(L + "w2", [128, 5632], BF16, F_W2)
    zrd = p.zmark
    DN = 1.0 / D
    THS = [1, 0]

    @blk.gpsimd
    def _(g):
        g.wait_ge(vv.h, zrd)
        for ti, th in enumerate(THS):
            for i in range(22):
                w = 128 if i < 21 else 42
                g.wait_ge(mm.h, S.at(L + f"w1done{th}", i - 2))
                b = (i % 2) * 2048
                wg.inc(g.dma_start(
                    out=w1sb[:, b:b + 2048].rearrange("p (c m) -> p c m", m=256)[:, :, 0:w],
                    in_=wslice(dr["iW1"], li, i * 128, w)), 16)
                wg.inc(g.dma_start(
                    out=w1sb[:, b:b + 2048].rearrange("p (c m) -> p c m", m=256)[:, :, 128:128 + w],
                    in_=wslice(dr["iW1"], li, DI + i * 128, w)), 16)
                p.ms(L + f"w1_{th}", wg.n)
            for dout in range(8):
                g.wait_ge(mm.h, S.at(L + f"w2done{th}", dout - 2))
                b = (dout % 2) * 2816
                wg.inc(g.dma_start(
                    out=w2sb[:, b:b + 2688].rearrange("p (c m) -> p c m", m=128),
                    in_=wslice(dr["iW2"], li, dout * 128, 128, row0=0, nrow=2688)), 16)
                wg.inc(g.dma_start(
                    out=w2sb[0:42, b + 2688:b + 2816],
                    in_=dr["iW2"][li, 2688:2730, dout * 128:(dout + 1) * 128]), 16)
                p.ms(L + f"w2_{th}", wg.n)

    for th in THS:
        @blk.tensor
        def _(t, th=th):
            if th == THS[0]:
                t.wait_ge(vv.h, zrd)
                t.wait_ge(ss.h, ss.n)
            for tc2 in range(2):
                tc = th * 2 + tc2
                j = th * 2 + tc2
                t.wait_ge(vv.h, S.at(L + "xnc", {1: tc2 - 1, 0: 2 + tc2 - 1}[th]))
                for kc in range(8):
                    inst = t.matmul(p.B[2][0:1, :], p.ones_col[:],
                                    z_ap(p, kc, tc * 512, 512, shift=(kc >= 4)),
                                    start=(kc == 0), stop=(kc == 7))
                mm.inc(inst)
                p.ms(L + "srow", mm.n)
                for kc in range(8):
                    t.wait_ge(vv.h, S.at(L + "zz", (2 * (1 - th) + tc2) * 8 + kc))
                    mm.inc(t.matmul(p.B[3][0:1, :], p.ones_col[:], zz[:],
                                    start=(kc == 0), stop=(kc == 7)))
                    p.ms(L + "zzmm", mm.n)
                p.ms(L + "ssrow", mm.n)
                t.wait_ge(vv.h, S.at(L + "murow", 2 * (1 - th) + tc2))
                mm.inc(t.matmul(p.B[4][:], p.ones_rowb[:],
                                rowsb[0:1, 0:512], start=True, stop=True))
                t.wait_ge(ss.h, S.at(L + "rstdrow", 2 * (1 - th) + tc2))
                mm.inc(t.matmul(p.B[5][:], p.ones_rowb[:],
                                rowsb[0:1, 512:1024], start=True, stop=True))
                p.ms(L + "bcast", mm.n)

        @blk.vector
        def _(v, th=th):
            for tc2 in range(2):
                tc = th * 2 + tc2
                gi = 2 * (1 - th) + tc2
                for kc in range(8):
                    if gi * 8 + kc >= 1:
                        v.wait_ge(mm.h, S.at(L + "zzmm", gi * 8 + kc - 1))
                    vv.inc(v.tensor_mul(zz[:], z_ap(p, kc, tc * 512, 512, shift=(kc >= 4)),
                                        z_ap(p, kc, tc * 512, 512, shift=(kc >= 4))))
                    p.ms(L + "zz", vv.n)
                v.wait_ge(mm.h, p.pv(L + "srow", gi))
                vv.inc(v.tensor_scalar_mul(p.stat[0:1, 0:512], p.B[2][0:1, :], DN))
                v.wait_ge(vv.h, vv.n)
                vv.inc(v.tensor_copy(rowsb[0:1, 0:512], p.stat[0:1, 0:512]))
                p.ms(L + "murow", vv.n)
                vv.inc(v.tensor_mul(p.stat[32:33, 0:512], p.stat[0:1, 0:512],
                                    p.stat[0:1, 0:512]))
                v.wait_ge(vv.h, vv.n)
                v.wait_ge(mm.h, p.pv(L + "ssrow", gi))
                vv.inc(v.scalar_tensor_tensor(p.stat[64:65, 0:512], p.B[3][0:1, :], DN,
                                              p.stat[32:33, 0:512], ALU.mult, ALU.subtract))
                v.wait_ge(vv.h, vv.n)
                p.ms(L + "varrow", vv.n)
                v.wait_ge(mm.h, p.pv(L + "bcast", gi))
                for kc in range(8):
                    vv.inc(v.scalar_tensor_tensor(
                        fscr[:], z_ap(p, kc, tc * 512, 512, shift=(kc >= 4)),
                        1.0, p.B[4][:], ALU.mult, ALU.subtract))
                    v.wait_ge(vv.h, vv.n)
                    vv.inc(v.tensor_mul(fscr[:], fscr[:], p.B[5][:]))
                    v.wait_ge(vv.h, vv.n)
                    vv.inc(v.tensor_scalar(
                        xn[:, kc * 1024 + tc2 * 512:kc * 1024 + tc2 * 512 + 512],
                        fscr[:], p.pvec[:, o + kc:o + kc + 1],
                        p.pvec[:, o + 8 + kc:o + 8 + kc + 1], ALU.mult, ALU.add))
                p.ms(L + "xnc", vv.n)
            p.ms(L + f"xnall{th}", vv.n)

        @blk.scalar
        def _(s, th=th):
            for tc2 in range(2):
                gi = 2 * (1 - th) + tc2
                s.wait_ge(vv.h, p.pv(L + "varrow", gi))
                ss.inc(s.activation(p.stat[96:97, 0:512], p.stat[64:65, 0:512], AF.Ln,
                                    bias=p.pvec[0:1, PV_C15:PV_C15 + 1]))
                s.wait_ge(ss.h, ss.n)
                ss.inc(s.activation(rowsb[0:1, 512:1024], p.stat[96:97, 0:512],
                                    AF.Exp, scale=p.pvec[0:1, PV_CM05:PV_CM05 + 1]))
                p.ms(L + "rstdrow", ss.n)

        @blk.tensor
        def _(t, th=th):
            t.wait_ge(vv.h, p.pv(L + f"xnall{th}"))
            for i in range(22):
                w = 128 if i < 21 else 42
                t.wait_ge(wg.h, p.pv(L + f"w1_{th}", i))
                t.wait_ge(vv.h, S.at(L + f"prodev{th}", (i - 1) * 2 + 1))
                b = (i % 2) * 2048
                for side in range(2):
                    for tokc in range(2):
                        bank = p.B[side * 2 + tokc]
                        for kc in range(8):
                            inst = t.matmul(
                                bank[0:w, :],
                                w1sb[:, b + kc * 256 + side * 128:b + kc * 256 + side * 128 + w],
                                xn[:, kc * 1024 + tokc * 512:kc * 1024 + tokc * 512 + 512],
                                start=(kc == 0), stop=(kc == 7))
                        mm.inc(inst)
                        p.ms(L + f"w1mm{th}", mm.n)
                p.ms(L + f"w1done{th}", mm.n)

        @blk.scalar
        def _(s, th=th):
            for i in range(22):
                w = 128 if i < 21 else 42
                for tokc in range(2):
                    s.wait_ge(mm.h, p.pv(L + f"w1mm{th}", i * 4 + 2 + tokc))
                    ss.inc(s.activation(gelu[0:w, tokc * 512:(tokc + 1) * 512],
                                        p.B[2 + tokc][0:w, :], AF.Gelu,
                                        bias=p.pvec[0:w, o + 38 + i:o + 39 + i]))
                    p.ms(L + f"gelu{th}", ss.n)

        @blk.vector
        def _(v, th=th):
            for i in range(22):
                w = 128 if i < 21 else 42
                for tokc in range(2):
                    v.wait_ge(mm.h, p.pv(L + f"w1mm{th}", i * 4 + tokc))
                    v.wait_ge(ss.h, p.pv(L + f"gelu{th}", i * 2 + tokc))
                    vv.inc(v.scalar_tensor_tensor(
                        prod[0:w, i * 1024 + tokc * 512:i * 1024 + tokc * 512 + 512],
                        p.B[tokc][0:w, :], p.pvec[0:w, o + 16 + i:o + 17 + i],
                        gelu[0:w, tokc * 512:(tokc + 1) * 512],
                        ALU.add, ALU.mult))
                    p.ms(L + f"prodev{th}", vv.n)
            p.ms(L + f"prodall{th}", vv.n)

        @blk.tensor
        def _(t, th=th):
            t.wait_ge(vv.h, p.pv(L + f"prodall{th}"))
            for dout in range(8):
                t.wait_ge(wg.h, p.pv(L + f"w2_{th}", dout))
                b = (dout % 2) * 2816
                for tokc in range(2):
                    bank = p.B[4 + ((dout * 2 + tokc) % 2)]
                    if dout * 2 + tokc >= 2:
                        t.wait_ge(vv.h, S.at(L + f"w2ev{th}", dout * 2 + tokc - 2))
                    for kc in range(22):
                        wk = 128 if kc < 21 else 42
                        inst = t.matmul(
                            bank[:],
                            w2sb[0:wk, b + kc * 128:b + kc * 128 + 128],
                            prod[0:wk, kc * 1024 + tokc * 512:kc * 1024 + tokc * 512 + 512],
                            start=(kc == 0), stop=(kc == 21))
                    mm.inc(inst)
                    p.ms(L + f"w2mm{th}", mm.n)
                p.ms(L + f"w2done{th}", mm.n)

        @blk.vector
        def _(v, th=th):
            for dout in range(8):
                for tokc in range(2):
                    v.wait_ge(mm.h, p.pv(L + f"w2mm{th}", dout * 2 + tokc))
                    zsl = z_ap(p, dout, th * 1024 + tokc * 512, 512)
                    vv.inc(v.scalar_tensor_tensor(
                        zsl, p.B[4 + ((dout * 2 + tokc) % 2)][:],
                        p.pvec[:, o + 60 + dout:o + 61 + dout], zsl,
                        ALU.add, ALU.add))
                    p.ms(L + f"w2ev{th}", vv.n)
            p.ms(L + f"fdone{th}", vv.n)
    p.zmark = p.sem["vv"].n



def emit_ln_rows(p, uid, srcsl, zzbuf, onesap, murow, rstdrow,
                 negmurow=None, stdrow=None, gate_vv=None):
    blk, S = p.blk, p.S
    mm, vv, ss = (p.sem[k] for k in ["mm", "vv", "ss"])

    @blk.tensor
    def _(t):
        if gate_vv is not None:
            t.wait_ge(vv.h, gate_vv)
        for tc in range(NTC):
            t.wait_ge(vv.h, S.at(uid + "rowsv", tc - 1))
            for kc in range(8):
                inst = t.matmul(p.B[2][0:1, :], onesap, srcsl(kc, tc),
                                start=(kc == 0), stop=(kc == 7))
            mm.inc(inst)
            p.ms(uid + "sr", mm.n)
            for kc in range(8):
                t.wait_ge(vv.h, S.at(uid + "zz", tc * 8 + kc))
                mm.inc(t.matmul(p.B[3][0:1, :], onesap, zzbuf[:],
                                start=(kc == 0), stop=(kc == 7)))
                p.ms(uid + "zzmm", mm.n)
            p.ms(uid + "ssr", mm.n)

    @blk.vector
    def _(v):
        if gate_vv is not None:
            v.wait_ge(vv.h, gate_vv)
        for tc in range(NTC):
            for kc in range(8):
                if tc * 8 + kc >= 1:
                    v.wait_ge(mm.h, S.at(uid + "zzmm", tc * 8 + kc - 1))
                sl = srcsl(kc, tc)
                vv.inc(v.tensor_mul(zzbuf[:], sl, sl))
                p.ms(uid + "zz", vv.n)
            v.wait_ge(mm.h, p.pv(uid + "sr", tc))
            vv.inc(v.tensor_scalar_mul(p.stat[0:1, 0:512], p.B[2][0:1, :], 1.0 / D))
            v.wait_ge(vv.h, vv.n)
            vv.inc(v.tensor_copy(murow(tc), p.stat[0:1, 0:512]))
            if negmurow is not None:
                vv.inc(v.tensor_scalar_mul(negmurow(tc), p.stat[0:1, 0:512], -1.0))
            vv.inc(v.tensor_mul(p.stat[32:33, 0:512], p.stat[0:1, 0:512],
                                p.stat[0:1, 0:512]))
            v.wait_ge(vv.h, vv.n)
            v.wait_ge(mm.h, p.pv(uid + "ssr", tc))
            vv.inc(v.scalar_tensor_tensor(p.stat[64:65, 0:512], p.B[3][0:1, :], 1.0 / D,
                                          p.stat[32:33, 0:512], ALU.mult, ALU.subtract))
            v.wait_ge(vv.h, vv.n)
            p.ms(uid + "var", vv.n)
            p.ms(uid + "rowsv", vv.n)

    @blk.scalar
    def _(s):
        for tc in range(NTC):
            s.wait_ge(vv.h, p.pv(uid + "var", tc))
            ss.inc(s.activation(p.stat[96:97, 0:512], p.stat[64:65, 0:512], AF.Ln,
                                bias=p.pvec[0:1, PV_C15:PV_C15 + 1]))
            s.wait_ge(ss.h, ss.n)
            inst = s.activation(rstdrow(tc), p.stat[96:97, 0:512], AF.Exp,
                                scale=p.pvec[0:1, PV_CM05:PV_CM05 + 1])
            if stdrow is not None:
                inst = s.activation(stdrow(tc), p.stat[96:97, 0:512], AF.Exp,
                                    scale=p.pvec[0:1, PV_P05:PV_P05 + 1])
            ss.inc(inst)
            p.ms(uid + "rows", ss.n)


def emit_ln_apply(p, uid, mode, murow, rstdrow, fscr, usrc=None, x1r=None,
                  gate_wh=None):
    blk, S = p.blk, p.S
    wh, mm, vv, ss = (p.sem[k] for k in ["wh", "mm", "vv", "ss"])

    @blk.tensor
    def _(t):
        for tc in range(NTC):
            t.wait_ge(ss.h, p.pv(uid + "rows", tc))
            t.wait_ge(vv.h, S.at(uid + "app", tc - 1))
            mm.inc(t.matmul(p.B[4][:], p.ones_rowb[:], murow(tc), start=True, stop=True))
            mm.inc(t.matmul(p.B[5][:], p.ones_rowb[:], rstdrow(tc), start=True, stop=True))
            p.ms(uid + "bc", mm.n)

    @blk.vector
    def _(v):
        if gate_wh is not None:
            v.wait_ge(wh.h, gate_wh)
        for tc in range(NTC):
            v.wait_ge(mm.h, p.pv(uid + "bc", tc))
            for kc in range(8):
                zsl = z_ap(p, kc, tc * 512, 512)
                src = usrc(kc, tc) if mode == "addu" else zsl
                vv.inc(v.scalar_tensor_tensor(fscr[:], src, 1.0, p.B[4][:],
                                              ALU.mult, ALU.subtract))
                v.wait_ge(vv.h, vv.n)
                if mode == "addu":
                    vv.inc(v.tensor_mul(fscr[:], fscr[:], p.B[5][:]))
                    v.wait_ge(vv.h, vv.n)
                    vv.inc(v.scalar_tensor_tensor(zsl, fscr[:], 1.0, zsl,
                                                  ALU.mult, ALU.add))
                elif mode == "norm":
                    vv.inc(v.tensor_mul(zsl, fscr[:], p.B[5][:]))
                else:
                    vv.inc(v.tensor_mul(fscr[:], fscr[:], p.B[5][:]))
                    v.wait_ge(vv.h, vv.n)
                    vv.inc(v.scalar_tensor_tensor(
                        zsl, fscr[:], 1.0,
                        x1r[:, kc * 2048 + tc * 512:kc * 2048 + tc * 512 + 512],
                        ALU.mult, ALU.add))
            p.ms(uid + "app", vv.n)
    p.zmark = p.sem["vv"].n


O_Y = 0
O_QP = 32768
O_KP = 36864
O_VP = 40960
O_S = 45056
O_P = 53248
O_PT = 57344
O_S0 = 65536
O_S1 = 67584
O_SH = 69632
O_RB = 71680
O_WB = 72704
O_W = 73728
O_U = 32768
O_ROWS = 86016
O_ZZ = O_ROWS + 12288
O_FS = O_ZZ + 2048


def emit_outer(p):
    blk, dr, S = p.blk, p.dr, p.S
    wh, wg, mm, vv, ss = (p.sem[k] for k in ["wh", "wg", "mm", "vv", "ss"])
    ar = p.arena
    y_buf = ar("Oy", [128, 16384], BF16, O_Y)
    qp = ar("Oqp", [128, 2048], BF16, O_QP)
    kp = ar("Okp", [128, 2048], BF16, O_KP)
    vp = ar("Ovp", [128, 2048], BF16, O_VP)
    s_sb = ar("Osb", [128, 2048], F32, O_S)
    p_sb = ar("Opb", [128, 2048], BF16, O_P)
    ptb = ar("Optb", [128, 4096], BF16, O_PT)
    s0 = ar("Os0", [128, 512], F32, O_S0)
    s1 = ar("Os1", [128, 512], F32, O_S1)
    sh = ar("Osh", [128, 512], F32, O_SH)
    rb = ar("Orb", [128, 512], BF16, O_RB)
    wbar = ar("Owb", [1, 384], BF16, O_WB)
    wqkv = ar("Owq", [128, 3072], F32R, O_W)
    wproj = ar("Owp", [128, 2048], BF16, O_W)
    u_buf = ar("Ou", [128, 16384], BF16, O_U)
    negmu = ar("Onm", [1, 2048], BF16, O_ROWS)
    rstdr = ar("Ors", [1, 2048], BF16, O_ROWS + 4096)
    stdr = ar("Ost", [1, 2048], BF16, O_ROWS + 8192)
    zz = ar("Ozz", [128, 512], F32R, O_ZZ)
    uz = ar("Ouz", [128, 512], BF16, O_ZZ)
    fscr = ar("Ofs", [128, 512], F32, O_FS)
    zrd = p.zmark

    emit_ln_rows(p, "oL0", lambda kc, tc: z_ap(p, kc, tc * 512, 512),
                 zz, p.ones_col[:],
                 lambda tc: negmu[0:1, tc * 512:(tc + 1) * 512],
                 lambda tc: rstdr[0:1, tc * 512:(tc + 1) * 512],
                 stdrow=lambda tc: stdr[0:1, tc * 512:(tc + 1) * 512],
                 gate_vv=zrd)

    @blk.vector
    def _(v):
        v.wait_ge(p.sem["ss"].h, p.pv("oL0rows", NTC - 1))
        vv.inc(v.tensor_scalar_mul(negmu[:], negmu[:], -1.0))
        p.ms("oL0negmu", vv.n)

    @blk.tensor
    def _(t):
        for tt in range(NTT):
            t.wait_ge(ss.h, p.pv("oL0rows", tt // 4))
            t.wait_ge(vv.h, S.at("oL0colv", tt - 2))
            mm.inc(t.matmul(p.PTB[tt % 2][:, 0:1],
                            rstdr[0:1, tt * 128:(tt + 1) * 128],
                            p.ones_rowb[0:1, 0:1], start=True, stop=True))
            p.ms("oL0coltr", mm.n)

    @blk.vector
    def _(v):
        for tt in range(NTT):
            v.wait_ge(mm.h, p.pv("oL0coltr", tt))
            bank = p.PTB[tt % 2]
            vv.inc(v.tensor_scalar_mul(p.stat[:, 16 + tt:17 + tt], bank[:, 0:1], 0.125))
            vv.inc(v.tensor_scalar_mul(p.stat[:, 32 + tt:33 + tt], bank[:, 0:1], 1.0))
            p.ms("oL0colv", vv.n)
        p.ms("oL0cols", vv.n)

    @blk.sync
    def _(s):
        for pr in range(8):
            if pr == 0:
                s.wait_ge(vv.h, zrd)
            s.wait_ge(mm.h, S.at("owqfree", pr - 1))
            for part in range(3):
                wh.inc(s.dma_start(
                    out=wqkv[:, part * 1024:(part + 1) * 1024]
                        .rearrange("p (c m) -> p c m", m=128),
                    in_=wslice2(dr["oWqkvT"], part * 1024 + pr * 128, 128)), 16)
            p.ms("owq", wh.n)

    for pr in range(8):
        emit_outer_pair(p, pr, y_buf, qp, kp, vp, s_sb, p_sb, ptb,
                        s0, s1, sh, rb, wbar, wqkv, negmu, rstdr, stdr)
    p.ms("oydone", vv.n)

    if _CFG["ydump"]:
        oydr = p.nc.dram_tensor("oydump", [128, 16384], BF16, kind="ExternalOutput").ap()
        oqdr = p.nc.dram_tensor("oqdump", [128, 2048], BF16, kind="ExternalOutput").ap()
        okdr = p.nc.dram_tensor("okdump", [128, 2048], BF16, kind="ExternalOutput").ap()
        ovdr = p.nc.dram_tensor("ovdump", [128, 2048], BF16, kind="ExternalOutput").ap()
        opdr = p.nc.dram_tensor("opdump", [128, 2048], BF16, kind="ExternalOutput").ap()
        osdr = p.nc.dram_tensor("osdump", [128, 2048], F32, kind="ExternalOutput").ap()
        optdr = p.nc.dram_tensor("optdump", [128, 4096], BF16, kind="ExternalOutput").ap()
        st_ = p.sem["st"]
        ym_ = p.pv("oydone")

        @blk.sync
        def _(s):
            s.wait_ge(vv.h, ym_)
            st_.inc(s.dma_start(out=oydr[:], in_=y_buf[:]), 16)
            st_.inc(s.dma_start(out=oqdr[:], in_=qp[:]), 16)
            st_.inc(s.dma_start(out=okdr[:], in_=kp[:]), 16)
            st_.inc(s.dma_start(out=ovdr[:], in_=vp[:]), 16)
            st_.inc(s.dma_start(out=opdr[:], in_=p_sb[:]), 16)
            st_.inc(s.dma_start(out=osdr[:], in_=s_sb[:]), 16)
            st_.inc(s.dma_start(out=optdr[:], in_=ptb[:]), 16)
            p.ms("oydump", st_.n)

        @blk.vector
        def _(v):
            v.wait_ge(st_.h, p.pv("oydump"))

    @blk.gpsimd
    def _(g):
        g.wait_ge(mm.h, p.pv("owqfree", 7))
        for dout in range(8):
            g.wait_ge(mm.h, S.at("owpdone", dout - 2))
            wg.inc(g.dma_start(
                out=wproj[:, (dout % 2) * 1024:(dout % 2) * 1024 + 1024]
                    .rearrange("p (c m) -> p c m", m=128),
                in_=wslice2(dr["oWprojT"], dout * 128, 128)), 16)
            p.ms("owp", wg.n)

    @blk.tensor
    def _(t):
        t.wait_ge(vv.h, p.pv("oydone"))
        t.wait_ge(ss.h, ss.n)
        for dout in range(8):
            t.wait_ge(wg.h, p.pv("owp", dout))
            for tc in range(NTC):
                k6 = (dout * 4 + tc) % 6
                if dout * 4 + tc >= 6:
                    t.wait_ge(vv.h, S.at("ouev", dout * 4 + tc - 6))
                for kc in range(8):
                    inst = t.matmul(
                        p.B[k6][:],
                        wproj[:, (dout % 2) * 1024 + kc * 128:(dout % 2) * 1024 + kc * 128 + 128],
                        y_buf[:, kc * 2048 + tc * 512:kc * 2048 + tc * 512 + 512],
                        start=(kc == 0), stop=(kc == 7))
                mm.inc(inst)
                p.ms("owpmm", mm.n)
            p.ms("owpdone", mm.n)

    @blk.vector
    def _(v):
        for dout in range(8):
            for tc in range(NTC):
                v.wait_ge(mm.h, p.pv("owpmm", dout * 4 + tc))
                vv.inc(v.tensor_scalar(
                    u_buf[:, dout * 2048 + tc * 512:dout * 2048 + tc * 512 + 512],
                    p.B[(dout * 4 + tc) % 6][:],
                    p.pvec[:, PV_OUT_BPROJ + dout:PV_OUT_BPROJ + dout + 1], None,
                    ALU.add))
                p.ms("ouev", vv.n)
        p.ms("oudone", vv.n)

    emit_ln_rows(p, "oLu", lambda kc, tc: u_buf[:, kc * 2048 + tc * 512:
                                                kc * 2048 + tc * 512 + 512],
                 uz, p.ones_colb[:],
                 lambda tc: negmu[0:1, tc * 512:(tc + 1) * 512],
                 lambda tc: rstdr[0:1, tc * 512:(tc + 1) * 512],
                 gate_vv=p.pv("oudone"))
    emit_ln_apply(p, "oLu", "addu",
                  lambda tc: negmu[0:1, tc * 512:(tc + 1) * 512],
                  lambda tc: rstdr[0:1, tc * 512:(tc + 1) * 512], fscr,
                  usrc=lambda kc, tc: u_buf[:, kc * 2048 + tc * 512:
                                            kc * 2048 + tc * 512 + 512])

    x1m = p.zmark

    @blk.sync
    def _(s):
        s.wait_ge(vv.h, x1m)
        for kc in range(8):
            wh.inc(s.dma_start(out=dr["x1_scr"][kc], in_=z_ap(p, kc, 0, T)), 16)
        p.ms("ox1sp", wh.n)

    emit_ln_rows(p, "oLi", lambda kc, tc: z_ap(p, kc, tc * 512, 512),
                 zz, p.ones_col[:],
                 lambda tc: negmu[0:1, tc * 512:(tc + 1) * 512],
                 lambda tc: rstdr[0:1, tc * 512:(tc + 1) * 512],
                 gate_vv=x1m)
    emit_ln_apply(p, "oLi", "norm",
                  lambda tc: negmu[0:1, tc * 512:(tc + 1) * 512],
                  lambda tc: rstdr[0:1, tc * 512:(tc + 1) * 512], fscr,
                  gate_wh=p.pv("ox1sp"))
    p.tab_last_read = max(p.tab_last_read, p.zmark)


def wslice2(w2d, col0, ncol):
    return w2d.rearrange("(c p) m -> p c m", p=128)[:, :, col0:col0 + ncol]


def emit_outer_pair(p, pr, y_buf, qp, kp, vp, s_sb, p_sb, ptb,
                    s0, s1, sh, rb, wbar, wqkv, negmu, rstdr, stdr):
    blk, S = p.blk, p.S
    wh, mm, vv, ss = (p.sem[k] for k in ["wh", "mm", "vv", "ss"])
    Lp = f"op{pr}_"
    SHUF = [(i + 1 if i % 2 == 0 else i - 1) for i in range(32)]

    @blk.tensor
    def _(t):
        t.wait_ge(wh.h, p.pv("owq", pr))
        if pr == 0:
            t.wait_ge(vv.h, p.pv("oL0cols"))
            t.wait_ge(vv.h, p.pv("oL0negmu"))
        else:
            t.wait_ge(vv.h, p.pv(f"op{pr-1}_yevall"))
        for part in range(3):
            for kc in range(8):
                inst = t.matmul(p.B[2][0:1, part * 128:(part + 1) * 128],
                                p.ones_col[:],
                                wqkv[:, part * 1024 + kc * 128:part * 1024 + kc * 128 + 128],
                                start=(kc == 0), stop=(kc == 7))
        mm.inc(inst)
        p.ms(Lp + "wbar", mm.n)
        for tc in range(NTC):
            t.wait_ge(vv.h, S.at(Lp + "qkv", tc * 2 - 2))
            t.wait_ge(vv.h, S.at(Lp + "rbfree", tc - 1))
            mm.inc(t.matmul(p.B[4][:], p.ones_rowb[:],
                            rstdr[0:1, tc * 512:(tc + 1) * 512], start=True, stop=True))
            p.ms(Lp + "rbc", mm.n)
            for side in range(2):
                bank = p.B[side]
                t.wait_ge(vv.h, S.at(Lp + "qkdone", tc * 2 + side - 2))
                for kc in range(8):
                    t.matmul(bank[:],
                             wqkv[:, side * 1024 + kc * 128:side * 1024 + kc * 128 + 128],
                             z_ap(p, kc, tc * 512, 512),
                             start=(kc == 0), stop=False)
                t.wait_ge(vv.h, S.at(Lp + "wbarv", 0))
                inst = t.matmul(bank[:], wbar[0:1, side * 128:side * 128 + 128],
                                negmu[0:1, tc * 512:(tc + 1) * 512],
                                start=False, stop=True)
                mm.inc(inst)
                p.ms(Lp + "qkmm", mm.n)
        for tt in range(NTT):
            bank = p.B[2 + tt % 2]
            if tt < 2:
                t.wait_ge(vv.h, p.pv(Lp + "wbarv"))
            t.wait_ge(vv.h, S.at(Lp + "vev", tt - 2))
            for kc in range(8):
                t.matmul(bank[:, 0:128], z_ap(p, kc, tt * 128, 128),
                         wqkv[:, 2048 + kc * 128:2048 + (kc + 1) * 128],
                         start=(kc == 0), stop=False)
            t.matmul(bank[:, 0:128], negmu[0:1, tt * 128:(tt + 1) * 128],
                     wbar[0:1, 256:384], start=False, stop=False)
            inst = t.matmul(bank[:, 0:128], stdr[0:1, tt * 128:(tt + 1) * 128],
                            p.brow_v[0:1, pr * 128:(pr + 1) * 128],
                            start=False, stop=True)
            mm.inc(inst)
            p.ms(Lp + "vmm", mm.n)
        p.ms("owqfree", mm.n)

    @blk.vector
    def _(v):
        v.wait_ge(mm.h, p.pv(Lp + "wbar"))
        vv.inc(v.tensor_copy(wbar[:], p.B[2][0:1, 0:384]))
        p.ms(Lp + "wbarv", vv.n)
        for tc in range(NTC):
            v.wait_ge(mm.h, p.pv(Lp + "rbc", tc))
            vv.inc(v.tensor_copy(rb[:], p.B[4][:]))
            v.wait_ge(vv.h, vv.n)
            p.ms(Lp + "rbfree", vv.n)
            for side in range(2):
                bank = p.B[side]
                v.wait_ge(mm.h, p.pv(Lp + "qkmm", tc * 2 + side))
                vv.inc(v.scalar_tensor_tensor(s0[:], bank[:], 1.0, rb[:],
                                              ALU.mult, ALU.mult))
                v.wait_ge(vv.h, vv.n)
                vv.inc(v.tensor_scalar_add(
                    s0[:], s0[:],
                    p.pvec[:, PV_OUT_BQKV + side * 8 + pr:PV_OUT_BQKV + side * 8 + pr + 1]))
                v.wait_ge(vv.h, vv.n)
                vv.inc(v.tensor_mul(s1[:], s0[:],
                                    p.tab[:, tc * 512:tc * 512 + 512]))
                vv.inc(v.stream_shuffle(sh[:], s0[:], SHUF))
                v.wait_ge(vv.h, vv.n)
                vv.inc(v.tensor_mul(sh[:], sh[:],
                                    p.tab[:, T + tc * 512:T + tc * 512 + 512]))
                v.wait_ge(vv.h, vv.n)
                dst = (qp if side == 0 else kp)[:, tc * 512:(tc + 1) * 512]
                vv.inc(v.tensor_add(dst, s1[:], sh[:]))
                p.ms(Lp + "qkdone", vv.n)
                p.ms(Lp + "qkv", vv.n)
        for tt in range(NTT):
            bank = p.B[2 + tt % 2]
            v.wait_ge(mm.h, p.pv(Lp + "vmm", tt))
            vv.inc(v.tensor_scalar_mul(vp[:, tt * 128:(tt + 1) * 128],
                                       bank[:, 0:128], p.stat[:, 32 + tt:33 + tt]))
            p.ms(Lp + "vev", vv.n)
        p.ms(Lp + "qkvall", vv.n)

    cnt = {"chunk": 0, "pt": 0, "pv": 0}

    @blk.tensor
    def _(t):
        t.wait_ge(vv.h, p.pv(Lp + "qkvall"))
        for hh in range(2):
            h0 = hh * 64
            kq = f"{Lp}h{hh}"
            for qt in range(NTT):
                nch = qt // 4 + 1
                for c5 in range(nch):
                    sb = p.B[5 if c5 % 2 == 0 else 0]
                    t.wait_ge(vv.h, S.at(Lp + "sev", cnt["chunk"] - 2))
                    if c5 == nch - 1:
                        t.matmul(sb[:], qp[h0:h0 + 64, qt * 128:(qt + 1) * 128],
                                 kp[h0:h0 + 64, c5 * 512:(c5 + 1) * 512],
                                 start=True, stop=False)
                        inst = t.matmul(sb[:], p.idb[:],
                                        p.omask[:, (qt % 4) * 512:(qt % 4) * 512 + 512],
                                        start=False, stop=True)
                    else:
                        inst = t.matmul(sb[:], qp[h0:h0 + 64, qt * 128:(qt + 1) * 128],
                                        kp[h0:h0 + 64, c5 * 512:(c5 + 1) * 512],
                                        start=True, stop=True)
                    mm.inc(inst)
                    p.ms(Lp + "smm", mm.n)
                    cnt["chunk"] += 1
                t.wait_ge(vv.h, S.at(Lp + "diag", hh * NTT + qt))
                for gci in range(nch):
                    t.wait_ge(vv.h, S.at(Lp + "ptev", cnt["pt"] - 2))
                    for c4 in range(4):
                        cell = gci * 4 + c4
                        inst = t.matmul(
                            p.PTB[gci % 2][:, c4 * 128:c4 * 128 + 128],
                            p_sb[:, cell * 128:cell * 128 + 128],
                            p.statb[:, (qt % 2) * 128:(qt % 2) * 128 + 128],
                            start=True, stop=True)
                    mm.inc(inst)
                    p.ms(Lp + "pt", mm.n)
                    cnt["pt"] += 1
                if qt % 2 == 1:
                    qg = qt // 2
                    t.wait_ge(vv.h, S.at(Lp + "ptevg", hh * 8 + qg))
                    t.wait_ge(vv.h, S.at(Lp + "yev", cnt["pv"] - 2))
                    yb = p.B[1 + (qg % 2)]
                    tp = (0, h0)
                    for kb in range(2 * qg + 2):
                        lo, hi = (128, 256) if kb == 2 * qg + 1 else (0, 256)
                        inst = t.matmul(
                            yb[h0:h0 + 64, lo:hi],
                            vp[:, kb * 128 + h0:kb * 128 + h0 + 64],
                            ptb[:, kb * 256 + lo:kb * 256 + hi],
                            start=(kb == 0), stop=(kb == 2 * qg + 1),
                            tile_position=tp)
                    mm.inc(inst)
                    p.ms(Lp + "pv", mm.n)
                    cnt["pv"] += 1

    cntv = {"chunk": 0, "pt": 0, "pv": 0}

    @blk.vector
    def _(v):
        for hh in range(2):
            h0 = hh * 64
            kq = f"{Lp}h{hh}"
            for qt in range(NTT):
                nch = qt // 4 + 1
                for c5 in range(nch):
                    sb = p.B[5 if c5 % 2 == 0 else 0]
                    v.wait_ge(mm.h, p.pv(Lp + "smm", cntv["chunk"]))
                    if qt >= 1 or hh >= 1:
                        v.wait_ge(ss.h, S.at(Lp + "exp", hh * NTT + qt - 1))
                    vv.inc(v.tensor_copy(s_sb[:, c5 * 512:(c5 + 1) * 512], sb[:]))
                    p.ms(Lp + "sev", vv.n)
                    cntv["chunk"] += 1
                L = nch * 512
                v.wait_ge(vv.h, vv.n)
                vv.inc(v.reduce_max(p.stat[:, qt % 4:qt % 4 + 1], s_sb[:, 0:L],
                                    axis=AX.X, negate=True))
                v.wait_ge(vv.h, vv.n)
                vv.inc(v.tensor_scalar_mul(p.stat[:, 4 + qt % 4:5 + qt % 4],
                                           p.stat[:, qt % 4:qt % 4 + 1], 0.125))
                p.ms(Lp + "bias", vv.n)
                v.wait_ge(ss.h, S.at(Lp + "rd", hh * NTT + qt))
                vv.inc(v.tensor_scalar_mul(
                    p.statb[:, (qt % 2) * 128:(qt % 2) * 128 + 128],
                    p.idb[:], p.stat[:, 12 + qt % 2:13 + qt % 2]))
                p.ms(Lp + "diag", vv.n)
                for gci in range(nch):
                    v.wait_ge(mm.h, p.pv(Lp + "pt", cntv["pt"]))
                    dstp = ptb.rearrange("k (a b) -> k a b", b=256)[
                        :, gci * 4:gci * 4 + 4, (qt % 2) * 128:(qt % 2) * 128 + 128]
                    vv.inc(v.tensor_copy(
                        dstp,
                        p.PTB[gci % 2][:, 0:512].rearrange("k (a b) -> k a b", b=128)))
                    p.ms(Lp + "ptev", vv.n)
                    cntv["pt"] += 1
                if qt % 2 == 1:
                    p.ms(Lp + "ptevg", vv.n)
                    qg = qt // 2
                    v.wait_ge(mm.h, p.pv(Lp + "pv", cntv["pv"]))
                    yb = p.B[1 + (qg % 2)]
                    vv.inc(v.tensor_copy(
                        y_buf[h0:h0 + 64, pr * 2048 + qg * 256:pr * 2048 + qg * 256 + 256],
                        yb[h0:h0 + 64, 0:256]))
                    p.ms(Lp + "yev", vv.n)
                    cntv["pv"] += 1
        p.ms(Lp + "yevall", vv.n)

    @blk.scalar
    def _(s):
        for hh in range(2):
            kq = f"{Lp}h{hh}"
            for qt in range(NTT):
                nch = qt // 4 + 1
                L = nch * 512
                s.wait_ge(vv.h, p.pv(Lp + "bias", hh * NTT + qt))
                ss.inc(s.activation(p_sb[:, 0:L], s_sb[:, 0:L], AF.Exp,
                                    scale=p.pvec[:, PV_C0125:PV_C0125 + 1],
                                    bias=p.stat[:, 4 + qt % 4:5 + qt % 4],
                                    accum_out=p.stat[:, 8 + qt % 4:9 + qt % 4]))
                p.ms(Lp + "exp", ss.n)
                s.wait_ge(ss.h, ss.n)
                ss.inc(s.activation(p.stat[:, 14:15],
                                    p.stat[:, 8 + qt % 4:9 + qt % 4], AF.Ln))
                s.wait_ge(ss.h, ss.n)
                ss.inc(s.activation(p.stat[:, 12 + qt % 2:13 + qt % 2],
                                    p.stat[:, 14:15], AF.Exp,
                                    scale=p.pvec[:, PV_CM1:PV_CM1 + 1]))
                p.ms(Lp + "rd", ss.n)



BF16_CONSTS = {"id128b", "e2", "e2t", "ones_rowb", "ones_colb", "zerosb",
               "zero_row", "omask", "imask", "otab_c", "otab_d",
               "itab_cq", "itab_sq", "itab_ck", "itab_sk"}


def kernel(**inputs):
    import ml_dtypes
    ins = {k: np.asarray(v) for k, v in inputs.items()}
    tabs = _host_tables()
    x_full = np.ascontiguousarray(ins["L0_attn_x"], dtype=np.float32)

    common = {
        "brow_v": ins["L0_attn_bqkv"].astype(np.float32)[OQKV_PERM][None, 2048:3072]
            .astype(ml_dtypes.bfloat16),
        "oWqkvT": np.ascontiguousarray(
            ins["L0_attn_Wqkv"].T.astype(np.float32)[:, OQKV_PERM]),
        "oWprojT": np.ascontiguousarray(ins["L0_attn_Wproj"].T.astype(np.float32)),
        "iWqkv": np.ascontiguousarray(ins["L0_ffn_Wqkv"].astype(np.float32)),
        "iWo": np.ascontiguousarray(ins["L0_ffn_Wo"].astype(np.float32)),
        "iW1": np.ascontiguousarray(ins["L0_ffn_W1"].astype(np.float32)),
        "iW2": np.ascontiguousarray(ins["L0_ffn_W2"].astype(np.float32)),
        "pvec": _build_pvec(ins),
        "id128b": tabs["id128"], "id128f": tabs["id128"], "id128r": tabs["id128"],
        "perm64": tabs["perm64"], "e2": tabs["e2"], "e2t": tabs["e2t"],
        "ones_col": tabs["ones_col"], "ones_row": tabs["ones_row"],
        "ones_rowb": tabs["ones_rowb"], "ones_colb": tabs["ones_colb"],
        "zerosb": tabs["zerosb"],
        "zero_row": tabs["zero_row"], "zeros128": tabs["zeros128"],
        "otab_c": tabs["otab_c"], "otab_d": tabs["otab_d"],
        "omask": tabs["omask"], "imask": tabs["imask"],
        "itab_cq": tabs["itab_cq"], "itab_sq": tabs["itab_sq"],
        "itab_ck": tabs["itab_ck"], "itab_sk": tabs["itab_sk"],
    }
    for k in list(common):
        if k in BF16_CONSTS:
            common[k] = common[k].astype(ml_dtypes.bfloat16)
    nc = build_nc()
    in_maps = []
    for c in range(NCORE):
        m = dict(common)
        m["x"] = np.ascontiguousarray(x_full[c])
        in_maps.append(m)
    res = run_bass_kernel_spmd(nc, in_maps, core_ids=list(range(NCORE)))
    out = np.stack([np.asarray(r["out"]) for r in res.results], axis=0)
    return out.astype(np.float32)
